# version 21
# baseline (speedup 1.0000x reference)
import math
from contextlib import ExitStack

import numpy as np
import ml_dtypes

import concourse.bass as bass
import concourse.mybir as mybir
from concourse.bass_utils import run_bass_kernel_spmd

F32 = mybir.dt.float32
BF16 = mybir.dt.bfloat16
I8 = mybir.dt.int8
AF = mybir.ActivationFunctionType
ALU = mybir.AluOpType
AX = mybir.AxisListType

S = 1024
NT = 8
NC = 32
NO = 32
NB = 2
NW = 4
NSG = 8
NSUP = NT * NSG
SCALE = 1.0 / math.sqrt(128.0)
EPS = 1e-5


class Cnt:
    def __init__(self, sem, step=1):
        self.sem, self.step, self.n = sem, step, 0

    def inc(self, ins):
        self.n += self.step
        ins.then_inc(self.sem, self.step)
        return self.n


def build_program(debug=False):
    nc = bass.Bass("TRN2", target_bir_lowering=False)

    x_d = nc.dram_tensor("x", [S, 4096], F32, kind="ExternalInput").ap()
    w_d, b_d = {}, {}
    for nm in ("q", "k", "v"):
        w_d[nm] = nc.dram_tensor(f"w{nm}", [NO, 128, NC, 128], F32, kind="ExternalInput").ap()
        b_d[nm] = nc.dram_tensor(f"b{nm}", [128, NO], F32, kind="ExternalInput").ap()
    kf_d = nc.dram_tensor("kf", [4, 128, 4096], I8, kind="ExternalInput").ap()
    out_d = nc.dram_tensor("out", [S, 4096], F32, kind="ExternalOutput").ap()

    scratch_kind = "ExternalOutput" if debug else "Internal"
    qT_d = nc.dram_tensor("qT_scr", [NT, 128, 32, 128], BF16, kind=scratch_kind).ap()
    vT_d = nc.dram_tensor("vT_scr", [NT, 128, 32, 128], BF16, kind=scratch_kind).ap()
    cxT_d = nc.dram_tensor("cxT_scr", [NT, 32, 128, 128], BF16, kind=scratch_kind).ap()
    ln1_dump = nc.dram_tensor("ln1_dump", [128, 32768], BF16,
                              kind="ExternalOutput").ap() if debug else None
    x1_dump = nc.dram_tensor("x1_dump", [S, 4096], F32,
                             kind="ExternalOutput").ap() if debug else None
    xr_dump = nc.dram_tensor("xr_dump", [S, 512], F32,
                             kind="ExternalOutput").ap() if debug else None
    ffn_dump = nc.dram_tensor("ffn_dump", [S, 4096], F32,
                              kind="ExternalOutput").ap() if debug else None
    ident_dram = nc.inline_tensor(np.eye(128, dtype=ml_dtypes.bfloat16), "ident").ap()

    ctx = ExitStack()
    E = ctx.enter_context
    with ctx:
        big0 = E(nc.sbuf_tensor("big0", [128, 32768], BF16))
        big1 = E(nc.sbuf_tensor("big1", [128, 32768], BF16))
        big2 = E(nc.sbuf_tensor("big2", [128, 32768], BF16))
        ln1T = big0[:, :]
        qtile = [big0[:, 0:4096], big0[:, 4096:8192]]
        vtile = [big0[:, 8192:12288], big0[:, 12288:16384]]
        x1b = [big0.bitcast(F32)[:, 0:4096], big0.bitcast(F32)[:, 4096:8192]]
        ffnb = [big0.bitcast(F32)[:, 8192:12288], big0.bitcast(F32)[:, 12288:16384]]
        lntile = [big1[:, 0:4096], big1[:, 4096:8192]]
        kT = big1[:, :]
        kffn = big1[:, 0:16384]
        outb = [big1.bitcast(F32)[:, 8192:12288], big1.bitcast(F32)[:, 12288:16384]]
        xbuf = [big2.bitcast(F32)[:, 0:4096], big2.bitcast(F32)[:, 4096:8192]]
        wring = [big2[:, 16384 + 4096 * i:16384 + 4096 * (i + 1)] for i in range(4)]
        stage = [big2[:, 16384:20480], big2[:, 20480:24576]]
        ctxn = [big2[:, 16384:20480], big2[:, 20480:24576]]

        evac = [E(nc.sbuf_tensor(f"evac{i}", [128, 512], BF16)) for i in range(4)]
        sid = E(nc.sbuf_tensor("sid", [128, 128], BF16))
        bias_s = E(nc.sbuf_tensor("bias_s", [128, 3 * NO], F32))
        negmax = E(nc.sbuf_tensor("negmax", [128, 16], F32))
        zsum = E(nc.sbuf_tensor("zsum", [128, 16], F32))
        rz = E(nc.sbuf_tensor("rz", [128, 16], F32))
        probsS = [E(nc.sbuf_tensor(f"probsS{i}", [128, 128], BF16)) for i in range(4)]
        probs2S = [E(nc.sbuf_tensor(f"probs2S{i}", [128, 128], BF16)) for i in range(4)]
        probsTS = [E(nc.sbuf_tensor(f"probsTS{i}", [128, 128], BF16)) for i in range(4)]
        bdiagS = [E(nc.sbuf_tensor(f"bdiagS{i}", [128, 512], BF16)) for i in range(2)]
        vgathS = [big2[:, 24576 + 512 * i:24576 + 512 * (i + 1)] for i in range(2)]
        vgrpS = [big2[:, 25600 + 512 * i:25600 + 512 * (i + 1)] for i in range(2)]
        st6 = E(nc.sbuf_tensor("st6", [128, 48], F32))
        mv = E(nc.sbuf_tensor("mv", [128, 2], F32))
        rstd = E(nc.sbuf_tensor("rstd", [128, 1], F32))
        lbias = E(nc.sbuf_tensor("lbias", [128, 1], F32))
        ltmp = E(nc.sbuf_tensor("ltmp", [128, 1], F32))
        ceps = E(nc.sbuf_tensor("ceps", [128, 1], F32))
        xr_raw = E(nc.sbuf_tensor("xr_raw", [128, 512], F32))
        xr16 = E(nc.sbuf_tensor("xr16", [128, 512], BF16))
        xrT = E(nc.sbuf_tensor("xrT", [128, 512], BF16))
        rstd4b = E(nc.sbuf_tensor("rstd4b", [128, 2], F32))
        lb4b = E(nc.sbuf_tensor("lb4b", [128, 2], F32))

        psP = E(nc.psum_tensor("psP", [128, 2048], F32))
        psQ = E(nc.psum_tensor("psQ", [128, 1024], F32))
        psR = E(nc.psum_tensor("psR", [128, 2048], BF16))

        names = ["xld", "ln1c", "lnt", "lne", "wdma", "mmP", "evP",
                 "spP0", "spP1", "spP2", "spP3",
                 "aux", "qvtl", "scs", "mxs", "exs", "dv", "vts", "vgs",
                 "cxs", "stg", "cxst", "p5", "ffnmm", "ffnev", "outst",
                 "kld", "dve2", "actx", "xte", "bds", "pool5", "vgp"]
        sem = {nm: E(nc.semaphore(nm)) for nm in names}
        xld = Cnt(sem["xld"], 16)
        ln1c = Cnt(sem["ln1c"])
        lnt = Cnt(sem["lnt"])
        lne = Cnt(sem["lne"])
        wdma = Cnt(sem["wdma"], 16)
        mmP = Cnt(sem["mmP"])
        evP = Cnt(sem["evP"])
        spP4 = [Cnt(sem["spP%d" % e], 16) for e in range(4)]
        aux = Cnt(sem["aux"], 16)
        qvtl = Cnt(sem["qvtl"], 16)
        scs = Cnt(sem["scs"])
        mxs = Cnt(sem["mxs"])
        exs = Cnt(sem["exs"])
        dv = Cnt(sem["dv"])
        vts = Cnt(sem["vts"])
        vgs = Cnt(sem["vgs"])
        cxs = Cnt(sem["cxs"])
        stg = Cnt(sem["stg"])
        cxst = Cnt(sem["cxst"], 16)
        p5 = Cnt(sem["p5"])
        ffnmm = Cnt(sem["ffnmm"])
        ffnev = Cnt(sem["ffnev"])
        outst = Cnt(sem["outst"], 16)
        kld = Cnt(sem["kld"], 16)
        dve2 = Cnt(sem["dve2"])
        actx = Cnt(sem["actx"])
        xte = Cnt(sem["xte"])
        bds = Cnt(sem["bds"])
        pool5 = Cnt(sem["pool5"])
        vgp = Cnt(sem["vgp"])

        aux.inc(nc.sync.dma_start(sid[:, :], ident_dram))
        for i, nm in enumerate(("q", "k", "v")):
            aux.inc(nc.sync.dma_start(bias_s[:, NO * i:NO * (i + 1)], b_d[nm]))

        nc.vector.wait_ge(sem["aux"], 16 * 4)
        nc.vector.tensor_scalar(bias_s[:, 0:NO], bias_s[:, 0:NO],
                                SCALE, None, ALU.mult)
        dve2.inc(nc.vector.memset(ceps[:, :], EPS))
        BIAS_DONE = dve2.n

        for t in range(NT):
            if t >= 2:
                nc.sync.wait_ge(sem["ln1c"], t - 1)
            if xld.n:
                nc.sync.wait_ge(sem["xld"], xld.n)
            xld.inc(nc.sync.dma_start(xbuf[t % 2][:, :], x_d[128 * t:128 * (t + 1), :]))

        def emit_ln_stats(src_slices):
            last = None
            for c, sl in enumerate(src_slices):
                last = nc.vector.bn_stats(st6[:, 6 * c:6 * c + 6], sl)
            dve2.inc(last)
            nc.vector.wait_ge(sem["dve2"], dve2.n)
            dve2.inc(nc.vector.bn_aggr(mv[:, :], st6[:, 0:6 * len(src_slices)]))
            nc.vector.wait_ge(sem["dve2"], dve2.n)

        def emit_rstd():
            nc.scalar.wait_ge(sem["dve2"], dve2.n)
            nc.scalar.activation(ltmp[:, :], mv[:, 1:2], AF.Ln,
                                 bias=ceps[:, 0:1], scale=1.0)
            actx.inc(nc.scalar.activation(rstd[:, :], ltmp[:, :], AF.Exp,
                                          bias=0.0, scale=-0.5))

        nc.tensor.wait_ge(sem["aux"], 16 * 4)
        for t in range(NT):
            nc.vector.wait_ge(sem["xld"], 16 * (t + 1))
            emit_ln_stats([xbuf[t % 2][:, 512 * c:512 * (c + 1)] for c in range(8)])
            emit_rstd()
            nc.vector.wait_ge(sem["actx"], actx.n)
            nc.vector.tensor_scalar(lbias[:, :], mv[:, 0:1], rstd[:, :], -1.0,
                                    ALU.mult, ALU.mult)
            if t >= 2:
                nc.vector.wait_ge(sem["lnt"], 8 * (t - 1))
            ln1c.inc(nc.vector.tensor_scalar(lntile[t % 2][:, :], xbuf[t % 2][:, :],
                                             rstd[:, :], lbias[:, :],
                                             ALU.mult, ALU.add))
            nc.tensor.wait_ge(sem["ln1c"], t + 1)
            for cb in range(8):
                B = 8 * t + cb
                bank = B % 2
                if B >= 2:
                    nc.tensor.wait_ge(sem["lne"], B - 1)
                for i in range(4):
                    c = 4 * cb + i
                    ins = nc.tensor.transpose(
                        psR[:, 1024 * bank + 128 * i:1024 * bank + 128 * (i + 1)],
                        lntile[t % 2][:, 128 * c:128 * (c + 1)], sid[:, :])
                lnt.inc(ins)
            for cb in range(8):
                B = 8 * t + cb
                bank = B % 2
                nc.scalar.wait_ge(sem["lnt"], B + 1)
                src = psR[:, 1024 * bank:1024 * bank + 512]
                dstap = ln1T[:, :].rearrange("p (c s) -> p c s", c=NC)[
                    :, 4 * cb:4 * cb + 4, 128 * t:128 * (t + 1)]
                lne.inc(nc.scalar.activation(dstap, src, AF.Copy))

        if debug:
            nc.sync.wait_ge(sem["lne"], 8 * NT)
            nc.sync.dma_start(ln1_dump, ln1T).then_inc(sem["aux"], 16)
        for w in range(96):
            if w >= NW:
                nc.gpsimd.wait_ge(sem["mmP"], NB * (w - NW + 1))
            if wdma.n:
                nc.gpsimd.wait_ge(sem["wdma"], wdma.n)
            src = w_d[("q", "k", "v")[w // 32]][w % 32].rearrange("p c o -> p (c o)")
            wdma.inc(nc.gpsimd.dma_start(wring[w % NW][:, :], src))

        g = 0
        for pi in range(3):
            for j in range(NO):
                w = pi * 32 + j
                nc.tensor.wait_ge(sem["wdma"], 16 * (w + 1))
                for b in range(NB):
                    if g >= 4:
                        nc.tensor.wait_ge(sem["evP"], g - 3)
                    nc.tensor.wait_ge(sem["lne"], 32 + 32 * b)
                    pslice = psP[:, 512 * (g % 4):512 * (g % 4 + 1)]
                    for c in range(NC):
                        ins = nc.tensor.matmul(
                            pslice,
                            wring[w % NW][:, 128 * c:128 * (c + 1)],
                            ln1T[:, 1024 * c + 512 * b:1024 * c + 512 * (b + 1)],
                            start=(c == 0), stop=(c == NC - 1))
                    mmP.inc(ins)
                    g += 1

        nc.scalar.wait_ge(sem["aux"], 16 * 4)
        nc.scalar.wait_ge(sem["dve2"], BIAS_DONE)
        nc.scalar.wait_ge(sem["lnt"], 8 * NT)
        g = 0
        nspill = 0
        for pi in range(3):
            for j in range(NO):
                for b in range(NB):
                    nc.scalar.wait_ge(sem["mmP"], g + 1)
                    pslice = psP[:, 512 * (g % 4):512 * (g % 4 + 1)]
                    bcol = bias_s[:, NO * pi + j:NO * pi + j + 1]
                    if pi == 1:
                        dstk = kT[:, 1024 * j + 512 * b:1024 * j + 512 * (b + 1)]
                        evP.inc(nc.scalar.activation(dstk, pslice, AF.Identity,
                                                     bias=bcol, scale=1.0))
                    else:
                        if nspill >= 4:
                            nc.scalar.wait_ge(sem["spP%d" % (nspill % 4)],
                                              16 * (nspill // 4))
                        eb = evac[nspill % 4]
                        sc = SCALE if pi == 0 else 1.0
                        evP.inc(nc.scalar.activation(eb[:, :], pslice, AF.Identity,
                                                     bias=bcol, scale=sc))
                        nc.scalar.wait_ge(sem["evP"], evP.n)
                        dst = (qT_d if pi == 0 else vT_d)[4 * b:4 * b + 4, :, j, :]
                        spP4[nspill % 4].inc(nc.scalar.dma_start(
                            dst.rearrange("st d s -> d st s"), eb[:, :]))
                        nspill += 1
                    g += 1

        nc.sync.wait_ge(sem["mmP"], 192)
        for e in range(4):
            nc.sync.wait_ge(sem["spP%d" % e], 16 * 32)
        for st in range(NT):
            if st >= 2:
                nc.sync.wait_ge(sem["scs"], NSG * (st - 1))
                nc.sync.wait_ge(sem["vgp"], NSG * (st - 1))
            if qvtl.n:
                nc.sync.wait_ge(sem["qvtl"], qvtl.n)
            qvtl.inc(nc.sync.dma_start(
                qtile[st % 2][:, :], qT_d[st].rearrange("d h s -> d (h s)")))
            nc.sync.wait_ge(sem["qvtl"], qvtl.n)
            qvtl.inc(nc.sync.dma_start(
                vtile[st % 2][:, :], vT_d[st].rearrange("d h s -> d (h s)")))

        nc.vector.memset(bdiagS[0][:, :], 0.0)
        dve2.inc(nc.vector.memset(bdiagS[1][:, :], 0.0))
        BDZ = dve2.n

        def dvc_recip(sg):
            return 4 * sg + 1

        def pe_scores(sg):
            st, sgi = divmod(sg, NSG)
            s0 = 16 * sgi
            par4 = sg % 4
            if sgi == 0:
                nc.tensor.wait_ge(sem["qvtl"], 16 * 2 * (st + 1))
                if st == 0:
                    nc.tensor.wait_ge(sem["evP"], 128)
                    nc.tensor.wait_ge(sem["dve2"], BDZ)
            if sg >= 4:
                nc.tensor.wait_ge(sem["exs"], sg - 3)
            scps = psP[:, 512 * par4:512 * par4 + 128]
            ins = None
            for g2 in range(4):
                for p in range(4):
                    sp_ = s0 + 4 * g2 + p
                    qs = qtile[st % 2].rearrange("d (h s) -> d s h", h=32)[:, sp_, :]
                    ks = kT.rearrange("d (t s) -> d s t", t=32)[:, 128 * st + sp_, :]
                    ins = nc.tensor.matmul(
                        scps[32 * p:32 * p + 32, 32 * g2:32 * g2 + 32], qs, ks,
                        start=True, stop=True, tile_position=(0, 32 * p))
            scs.inc(ins)

        def pe_vtrans(sg):
            par = sg % 2
            if sg >= 1:
                nc.tensor.wait_ge(sem["vgs"], sg)
            nc.tensor.wait_ge(sem["vgp"], sg + 1)
            ins = None
            for g2 in range(4):
                off = 1024 * (g2 // 2) + 128 * (g2 % 2)
                ins = nc.tensor.transpose(psR[:, off:off + 128],
                                          vgathS[par][:, 128 * g2:128 * (g2 + 1)],
                                          sid[:, :])
            vts.inc(ins)

        def pe_ctx(sg):
            par = sg % 2
            nc.tensor.wait_ge(sem["vgs"], sg + 1)
            nc.tensor.wait_ge(sem["dv"], dvc_recip(sg) + 2)
            if sg >= 2:
                nc.tensor.wait_ge(sem["stg"], sg - 1)
            cxps = psQ[:, 512 * par:512 * par + 512]
            ins = None
            for g2 in range(4):
                ins = nc.tensor.matmul(cxps[:, 128 * g2:128 * (g2 + 1)],
                                       vgrpS[par][:, 128 * g2:128 * (g2 + 1)],
                                       bdiagS[par][:, 128 * g2:128 * (g2 + 1)],
                                       start=True, stop=True)
            cxs.inc(ins)

        def pool_vgath(sg):
            st, sgi = divmod(sg, NSG)
            s0 = 16 * sgi
            par = sg % 2
            if sg >= 2:
                nc.gpsimd.wait_ge(sem["vts"], sg - 1)
            if sgi == 0:
                nc.gpsimd.wait_ge(sem["qvtl"], 16 * 2 * (st + 1))
            vsl = vtile[st % 2].rearrange("d (t s) -> d s t", t=32)[:, s0:s0 + 16, :]
            vgp.inc(nc.gpsimd.tensor_copy(vgathS[par], vsl))

        def dve_head(sg):
            par4 = sg % 4
            nc.vector.wait_ge(sem["scs"], sg + 1)
            if sg >= 4:
                nc.vector.wait_ge(sem["exs"], sg - 3)
            inap = psP[:, 512 * par4:512 * par4 + 128].rearrange(
                "p (g t) -> p g t", g=4)
            mxs.inc(nc.vector.reduce_max(
                negmax[:, 4 * par4:4 * par4 + 4].rearrange("p (g o) -> p g o", o=1),
                inap, axis=AX.X, negate=True))

        def dve_tail(sg):
            st, sgi = divmod(sg, NSG)
            s0 = 16 * sgi
            par = sg % 2
            par4 = sg % 4
            nc.vector.wait_ge(sem["exs"], sg + 1)
            dv.inc(nc.vector.reciprocal(rz[:, 4 * par4:4 * par4 + 4],
                                        zsum[:, 4 * par4:4 * par4 + 4]))
            nc.vector.wait_ge(sem["dv"], dvc_recip(sg))
            bc = rz[:, 4 * par4:4 * par4 + 4].rearrange(
                "p (g o) -> p g o", o=1).broadcast_to([128, 4, 32])
            dv.inc(nc.vector.tensor_tensor(
                probs2S[par4][:, :].rearrange("p (g t) -> p g t", g=4),
                probsS[par4][:, :].rearrange("p (g t) -> p g t", g=4),
                bc, ALU.mult))
            nc.vector.wait_ge(sem["dv"], dvc_recip(sg) + 1)
            dv.inc(nc.vector.transpose(probsTS[par4][:, :], probs2S[par4][:, :]))
            nc.vector.wait_ge(sem["dv"], dvc_recip(sg) + 2)
            if sg >= 2:
                nc.vector.wait_ge(sem["cxs"], sg - 1)
            ins = None
            for p in range(4):
                outap = bdiagS[par][32 * p:32 * p + 32, :].rearrange(
                    "q (g h) -> q g h", g=4)[:, :, 32 * p:32 * p + 32]
                ins = nc.vector.tensor_copy(
                    outap,
                    probsTS[par4][32 * p:32 * p + 32, :].rearrange(
                        "q (g h) -> q g h", g=4))
            dv.inc(ins)
            nc.vector.wait_ge(sem["vts"], sg + 1)
            invg = psR[:, :].rearrange("p (b c) -> p b c", b=2)[:, :, 0:256]
            vgs.inc(nc.vector.tensor_copy(
                vgrpS[par].rearrange("p (b c) -> p b c", b=2), invg))

        def act_stage_evac(sg):
            st, sgi = divmod(sg, NSG)
            s0 = 16 * sgi
            par = sg % 2
            nc.scalar.wait_ge(sem["cxs"], sg + 1)
            if sgi == 0 and st >= 2:
                nc.scalar.wait_ge(sem["cxst"], 16 * (st - 1))
            sout = stage[st % 2].rearrange("d (h s) -> d s h", h=32)[:, s0:s0 + 16, :]
            stg.inc(nc.scalar.activation(
                sout, psQ[:, 512 * par:512 * par + 512].rearrange(
                    "p (s h) -> p s h", s=16), AF.Copy))

        def act_exps(sg):
            par4 = sg % 4
            nc.scalar.wait_ge(sem["mxs"], sg + 1)
            if sg >= 4:
                nc.scalar.wait_ge(sem["dv"], dvc_recip(sg - 4) + 1)
            ins = None
            for g2 in range(4):
                ins = nc.scalar.activation(
                    probsS[par4][:, 32 * g2:32 * (g2 + 1)],
                    psP[:, 512 * par4 + 32 * g2:512 * par4 + 32 * (g2 + 1)],
                    AF.Exp, bias=negmax[:, 4 * par4 + g2:4 * par4 + g2 + 1],
                    scale=1.0, accum_out=zsum[:, 4 * par4 + g2:4 * par4 + g2 + 1])
            exs.inc(ins)

        for sg in range(NSUP + 1):
            if sg < NSUP:
                pool_vgath(sg)
                pe_scores(sg)
            if sg >= 1:
                pe_ctx(sg - 1)
            if sg < NSUP:
                pe_vtrans(sg)
                dve_head(sg)
            if sg >= 1:
                dve_tail(sg - 1)
            if sg < NSUP:
                act_exps(sg)
            if sg >= 1:
                act_stage_evac(sg - 1)
            if sg >= 1 and (sg - 1) % NSG == NSG - 1:
                st_done = (sg - 1) // NSG
                nc.scalar.wait_ge(sem["stg"], NSG * (st_done + 1))
                if cxst.n:
                    nc.scalar.wait_ge(sem["cxst"], cxst.n)
                cxst.inc(nc.scalar.dma_start(
                    cxT_d[st_done].transpose([1, 0, 2]), stage[st_done % 2][:, :]))

        nc.gpsimd.wait_ge(sem["scs"], NSUP)
        for c in range(4):
            kld.inc(nc.gpsimd.dma_start(kffn[:, 4096 * c:4096 * (c + 1)], kf_d[c]))

        nc.sync.wait_ge(sem["cxst"], 16 * NT)
        nc.vector.wait_ge(sem["cxst"], 16 * NT)
        nc.scalar.wait_ge(sem["cxst"], 16 * NT)
        nc.tensor.wait_ge(sem["cxst"], 16 * NT)
        nc.tensor.wait_ge(sem["kld"], 16 * 4)
        nc.gpsimd.wait_ge(sem["cxst"], 16 * NT)

        XLD0 = xld.n
        QV0 = qvtl.n
        for st in range(NT):
            if st >= 2:
                nc.sync.wait_ge(sem["p5"], 4 * (st - 2) + 2)
            nc.sync.wait_ge(sem["xld"], xld.n)
            xld.inc(nc.sync.dma_start(xbuf[st % 2][:, :],
                                      x_d[128 * st:128 * (st + 1), :]))
            nc.sync.wait_ge(sem["qvtl"], qvtl.n)
            qvtl.inc(nc.sync.dma_start(
                ctxn[st % 2][:, :], cxT_d[st].rearrange("h d s -> (h d) s"),
                transpose=True))

        VT0 = vts.n
        for st in range(NT):
            par = st % 2
            nc.vector.wait_ge(sem["qvtl"], QV0 + 16 * (st + 1))
            emit_ln_stats([ctxn[par][:, 512 * c:512 * (c + 1)] for c in range(8)])
            p5.inc(nc.vector.engine_nop())
            emit_rstd()
            nc.vector.wait_ge(sem["actx"], actx.n)
            dve2.inc(nc.vector.tensor_scalar(lbias[:, :], mv[:, 0:1], rstd[:, :],
                                             -1.0, ALU.mult, ALU.mult))
            nc.scalar.wait_ge(sem["dve2"], dve2.n)
            if st >= 2:
                nc.scalar.wait_ge(sem["p5"], 4 * (st - 2) + 4)
            actx.inc(nc.scalar.activation(ffnb[par][:, :], ctxn[par][:, :],
                                          AF.Identity, bias=lbias[:, :],
                                          scale=rstd[:, :]))
            nc.vector.wait_ge(sem["actx"], actx.n)
            nc.vector.wait_ge(sem["xld"], XLD0 + 16 * (st + 1))
            p5.inc(nc.vector.tensor_add(x1b[par][:, :], ffnb[par][:, :],
                                        xbuf[par][:, :]))
            if debug:
                nc.scalar.wait_ge(sem["p5"], 4 * st + 2)
                nc.scalar.dma_start(x1_dump[128 * st:128 * (st + 1), :],
                                    x1b[par][:, :]).then_inc(sem["aux"], 16)
                nc.vector.wait_ge(sem["aux"], 16 * (4 + 1 + 3 * st + 1))
            emit_ln_stats([x1b[par][:, 512 * c:512 * (c + 1)] for c in range(8)])
            emit_rstd()
            nc.gpsimd.wait_ge(sem["p5"], 4 * st + 2)
            nc.gpsimd.tensor_add(xr_raw[:, :], x1b[par][:, 0:512],
                                 x1b[par][:, 512:1024])
            for gf in range(2, 8):
                ins = nc.gpsimd.tensor_add(xr_raw[:, :], xr_raw[:, :],
                                           x1b[par][:, 512 * gf:512 * (gf + 1)])
            pool5.inc(ins)
            nc.vector.wait_ge(sem["actx"], actx.n)
            dve2.inc(nc.vector.tensor_scalar(lbias[:, :], mv[:, 0:1], rstd[:, :],
                                             -8.0, ALU.mult, ALU.mult))
            nc.vector.wait_ge(sem["pool5"], pool5.n)
            nc.vector.wait_ge(sem["dve2"], dve2.n)
            if st >= 1:
                nc.vector.wait_ge(sem["vts"], VT0 + 4 * st)
            p5.inc(nc.vector.tensor_scalar(xr16[:, :], xr_raw[:, :], rstd[:, :],
                                           lbias[:, :], ALU.mult, ALU.add))
            if debug:
                nc.scalar.wait_ge(sem["p5"], 4 * st + 3)
                nc.scalar.dma_start(xr_dump[128 * st:128 * (st + 1), :],
                                    xr_raw[:, :]).then_inc(sem["aux"], 16)
                nc.vector.wait_ge(sem["aux"], 16 * (4 + 1 + 3 * st + 2))
            nc.tensor.wait_ge(sem["p5"], 4 * st + 3)
            for c in range(4):
                T = 4 * st + c
                if T >= 2:
                    nc.tensor.wait_ge(sem["xte"], T - 1)
                tp = psR[:, 1024 * (c % 2):1024 * (c % 2) + 128]
                vts.inc(nc.tensor.transpose(tp, xr16[:, 128 * c:128 * (c + 1)],
                                            sid[:, :]))
                nc.scalar.wait_ge(sem["vts"], vts.n)
                if c == 0:
                    nc.scalar.wait_ge(sem["ffnmm"], 4 * st)
                xte.inc(nc.scalar.activation(xrT[:, 128 * c:128 * (c + 1)], tp,
                                             AF.Identity, bias=0.0, scale=1.0))
            nc.tensor.wait_ge(sem["xte"], 4 * (st + 1))
            for qq in range(4):
                QQ = 4 * st + qq
                qpar = qq % 2
                if QQ >= 2:
                    nc.tensor.wait_ge(sem["ffnev"], QQ - 1)
                for c in range(4):
                    for u2 in range(2):
                        base = 1024 * qpar + 512 * u2
                        ins = nc.tensor.matmul(
                            psP[:, base:base + 512],
                            xrT[:, 128 * c:128 * (c + 1)],
                            kffn[:, 4096 * c + 1024 * qq + 512 * u2:
                                 4096 * c + 1024 * qq + 512 * (u2 + 1)],
                            start=(c == 0), stop=(c == 3))
                ffnmm.inc(ins)
                nc.scalar.wait_ge(sem["ffnmm"], ffnmm.n)
                if qq == 0:
                    nc.scalar.wait_ge(sem["p5"], 4 * st + 2)
                    if st >= 2:
                        nc.scalar.wait_ge(sem["p5"], 4 * (st - 2) + 4)
                ffnev.inc(nc.scalar.activation(
                    ffnb[par][:, 1024 * qq:1024 * (qq + 1)],
                    psP[:, 1024 * qpar:1024 * qpar + 1024], AF.Copy))
            if debug:
                nc.scalar.dma_start(ffn_dump[128 * st:128 * (st + 1), :],
                                    ffnb[par][:, :]).then_inc(sem["aux"], 16)
                nc.vector.wait_ge(sem["aux"], 16 * (4 + 1 + 3 * st + 3))
            nc.vector.wait_ge(sem["ffnev"], ffnev.n)
            emit_ln_stats([ffnb[par][:, 512 * c:512 * (c + 1)] for c in range(8)])
            emit_rstd()
            nc.vector.wait_ge(sem["actx"], actx.n)
            nc.vector.tensor_scalar(lbias[:, :], mv[:, 0:1], rstd[:, :], -1.0,
                                    ALU.mult, ALU.mult)
            if st >= 2:
                nc.vector.wait_ge(sem["outst"], 16 * (st - 1))
            nc.vector.scalar_tensor_tensor(outb[par][:, :], ffnb[par][:, :],
                                           rstd[:, :], x1b[par][:, :],
                                           ALU.mult, ALU.add)
            p5.inc(nc.vector.tensor_scalar(outb[par][:, :], outb[par][:, :],
                                           lbias[:, :], None, ALU.add))
            nc.scalar.wait_ge(sem["p5"], 4 * st + 4)
            if outst.n:
                nc.scalar.wait_ge(sem["outst"], outst.n)
            outst.inc(nc.scalar.dma_start(out_d[128 * st:128 * (st + 1), :],
                                          outb[par][:, :]))

        nc.sync.wait_ge(sem["outst"], 16 * NT)
        if debug:
            nc.sync.wait_ge(sem["aux"], 16 * (4 + 1 + 3 * NT))
    return nc


_PROG_CACHE = {}


def kernel(L0_attn_hidden_states, L0_attn_wq, L0_attn_bq, L0_attn_wk, L0_attn_bk,
           L0_attn_wv, L0_attn_bv, L0_attn_wl, L0_attn_bl, L0_ffn_kernel):
    x = np.ascontiguousarray(np.asarray(L0_attn_hidden_states, dtype=np.float32)
                             .reshape(8192, 4096))

    def prep_w(w):
        w = np.asarray(w, dtype=np.float32)
        return np.ascontiguousarray(w.reshape(32, 128, 32, 128).transpose(0, 3, 2, 1))

    def prep_b(b):
        return np.ascontiguousarray(np.asarray(b, dtype=np.float32).reshape(32, 128).T)

    wq, wk, wv = prep_w(L0_attn_wq), prep_w(L0_attn_wk), prep_w(L0_attn_wv)
    bq, bk, bv = prep_b(L0_attn_bq), prep_b(L0_attn_bk), prep_b(L0_attn_bv)
    kf = np.ascontiguousarray(np.asarray(L0_ffn_kernel, dtype=np.int8)
                              .reshape(4, 128, 4096))

    if "nc" not in _PROG_CACHE:
        _PROG_CACHE["nc"] = build_program()
    nc = _PROG_CACHE["nc"]

    in_maps = [dict(x=np.ascontiguousarray(x[1024 * c:1024 * (c + 1)]),
                    wq=wq, wk=wk, wv=wv, bq=bq, bk=bk, bv=bv, kf=kf)
               for c in range(8)]
    res = run_bass_kernel_spmd(nc, in_maps, core_ids=list(range(8)))
    out = np.concatenate([np.asarray(r["out"]) for r in res.results], axis=0)
    return out.reshape(1, 8192, 4096).astype(np.float32)


# version 22
# speedup vs baseline: 1.0045x; 1.0045x over previous
import math
from contextlib import ExitStack

import numpy as np
import ml_dtypes

import concourse.bass as bass
import concourse.mybir as mybir
from concourse.bass_utils import run_bass_kernel_spmd

F32 = mybir.dt.float32
BF16 = mybir.dt.bfloat16
I8 = mybir.dt.int8
AF = mybir.ActivationFunctionType
ALU = mybir.AluOpType
AX = mybir.AxisListType

S = 1024
NT = 8
NC = 32
NO = 32
NB = 2
NW = 4
NSG = 8
NSUP = NT * NSG
SCALE = 1.0 / math.sqrt(128.0)
EPS = 1e-5


class Cnt:
    def __init__(self, sem, step=1):
        self.sem, self.step, self.n = sem, step, 0

    def inc(self, ins):
        self.n += self.step
        ins.then_inc(self.sem, self.step)
        return self.n


def build_program(debug=False):
    nc = bass.Bass("TRN2", target_bir_lowering=False)

    x_d = nc.dram_tensor("x", [S, 4096], F32, kind="ExternalInput").ap()
    w_d, b_d = {}, {}
    for nm in ("q", "k", "v"):
        w_d[nm] = nc.dram_tensor(f"w{nm}", [NO, 128, NC, 128], F32, kind="ExternalInput").ap()
        b_d[nm] = nc.dram_tensor(f"b{nm}", [128, NO], F32, kind="ExternalInput").ap()
    kf_d = nc.dram_tensor("kf", [4, 128, 4096], I8, kind="ExternalInput").ap()
    out_d = nc.dram_tensor("out", [S, 4096], F32, kind="ExternalOutput").ap()

    scratch_kind = "ExternalOutput" if debug else "Internal"
    qT_d = nc.dram_tensor("qT_scr", [NT, 128, 32, 128], BF16, kind=scratch_kind).ap()
    vT_d = nc.dram_tensor("vT_scr", [NT, 128, 32, 128], BF16, kind=scratch_kind).ap()
    cxT_d = nc.dram_tensor("cxT_scr", [NT, 32, 128, 128], BF16, kind=scratch_kind).ap()
    ln1_dump = nc.dram_tensor("ln1_dump", [128, 32768], BF16,
                              kind="ExternalOutput").ap() if debug else None
    x1_dump = nc.dram_tensor("x1_dump", [S, 4096], F32,
                             kind="ExternalOutput").ap() if debug else None
    xr_dump = nc.dram_tensor("xr_dump", [S, 512], F32,
                             kind="ExternalOutput").ap() if debug else None
    ffn_dump = nc.dram_tensor("ffn_dump", [S, 4096], F32,
                              kind="ExternalOutput").ap() if debug else None
    ident_dram = nc.inline_tensor(np.eye(128, dtype=ml_dtypes.bfloat16), "ident").ap()

    ctx = ExitStack()
    E = ctx.enter_context
    with ctx:
        big0 = E(nc.sbuf_tensor("big0", [128, 32768], BF16))
        big1 = E(nc.sbuf_tensor("big1", [128, 32768], BF16))
        big2 = E(nc.sbuf_tensor("big2", [128, 32768], BF16))
        ln1T = big0[:, :]
        qtile = [big0[:, 0:4096], big0[:, 4096:8192]]
        vtile = [big0[:, 8192:12288], big0[:, 12288:16384]]
        x1b = [big0.bitcast(F32)[:, 0:4096], big0.bitcast(F32)[:, 4096:8192]]
        ffnb = [big0.bitcast(F32)[:, 8192:12288], big0.bitcast(F32)[:, 12288:16384]]
        lntile = [big1[:, 0:4096], big1[:, 4096:8192]]
        kT = big1[:, :]
        kffn = big1[:, 0:16384]
        outb = [big1.bitcast(F32)[:, 8192:12288], big1.bitcast(F32)[:, 12288:16384]]
        xbuf = [big2.bitcast(F32)[:, 0:4096], big2.bitcast(F32)[:, 4096:8192]]
        wring = [big2[:, 16384 + 4096 * i:16384 + 4096 * (i + 1)] for i in range(4)]
        stage = [big2[:, 16384:20480], big2[:, 20480:24576]]
        ctxn = [big2[:, 16384:20480], big2[:, 20480:24576]]

        evac = [E(nc.sbuf_tensor(f"evac{i}", [128, 512], BF16)) for i in range(4)]
        sid = E(nc.sbuf_tensor("sid", [128, 128], BF16))
        bias_s = E(nc.sbuf_tensor("bias_s", [128, 3 * NO], F32))
        negmax = E(nc.sbuf_tensor("negmax", [128, 16], F32))
        zsum = E(nc.sbuf_tensor("zsum", [128, 16], F32))
        rz = E(nc.sbuf_tensor("rz", [128, 16], F32))
        probsS = [E(nc.sbuf_tensor(f"probsS{i}", [128, 128], BF16)) for i in range(4)]
        probs2S = [E(nc.sbuf_tensor(f"probs2S{i}", [128, 128], BF16)) for i in range(4)]
        probsTS = [E(nc.sbuf_tensor(f"probsTS{i}", [128, 128], BF16)) for i in range(4)]
        bdiagS = [E(nc.sbuf_tensor(f"bdiagS{i}", [128, 512], BF16)) for i in range(2)]
        vgathS = [big2[:, 24576 + 512 * i:24576 + 512 * (i + 1)] for i in range(2)]
        vgrpS = [big2[:, 25600 + 512 * i:25600 + 512 * (i + 1)] for i in range(2)]
        st6 = E(nc.sbuf_tensor("st6", [128, 48], F32))
        mv = E(nc.sbuf_tensor("mv", [128, 2], F32))
        rstd = E(nc.sbuf_tensor("rstd", [128, 1], F32))
        lbias = E(nc.sbuf_tensor("lbias", [128, 1], F32))
        ltmp = E(nc.sbuf_tensor("ltmp", [128, 1], F32))
        ceps = E(nc.sbuf_tensor("ceps", [128, 1], F32))
        xr_raw = E(nc.sbuf_tensor("xr_raw", [128, 512], F32))
        xr16 = E(nc.sbuf_tensor("xr16", [128, 512], BF16))
        xrT = E(nc.sbuf_tensor("xrT", [128, 512], BF16))
        rstd4b = E(nc.sbuf_tensor("rstd4b", [128, 2], F32))
        lb4b = E(nc.sbuf_tensor("lb4b", [128, 2], F32))

        psP = E(nc.psum_tensor("psP", [128, 2048], F32))
        psQ = E(nc.psum_tensor("psQ", [128, 1024], F32))
        psR = E(nc.psum_tensor("psR", [128, 2048], BF16))

        names = ["xld", "ln1c", "lnt", "lne", "wdma", "mmP", "evP",
                 "spP0", "spP1", "spP2", "spP3",
                 "aux", "qvtl", "scs", "mxs", "exs", "dv", "vts", "vgs",
                 "cxs", "stg", "cxst", "p5", "ffnmm", "ffnev", "outst",
                 "kld", "dve2", "actx", "xte", "bds", "pool5", "vgp"]
        sem = {nm: E(nc.semaphore(nm)) for nm in names}
        xld = Cnt(sem["xld"], 16)
        ln1c = Cnt(sem["ln1c"])
        lnt = Cnt(sem["lnt"])
        lne = Cnt(sem["lne"])
        wdma = Cnt(sem["wdma"], 16)
        mmP = Cnt(sem["mmP"])
        evP = Cnt(sem["evP"])
        spP4 = [Cnt(sem["spP%d" % e], 16) for e in range(4)]
        aux = Cnt(sem["aux"], 16)
        qvtl = Cnt(sem["qvtl"], 16)
        scs = Cnt(sem["scs"])
        mxs = Cnt(sem["mxs"])
        exs = Cnt(sem["exs"])
        dv = Cnt(sem["dv"])
        vts = Cnt(sem["vts"])
        vgs = Cnt(sem["vgs"])
        cxs = Cnt(sem["cxs"])
        stg = Cnt(sem["stg"])
        cxst = Cnt(sem["cxst"], 16)
        p5 = Cnt(sem["p5"])
        ffnmm = Cnt(sem["ffnmm"])
        ffnev = Cnt(sem["ffnev"])
        outst = Cnt(sem["outst"], 16)
        kld = Cnt(sem["kld"], 16)
        dve2 = Cnt(sem["dve2"])
        actx = Cnt(sem["actx"])
        xte = Cnt(sem["xte"])
        bds = Cnt(sem["bds"])
        pool5 = Cnt(sem["pool5"])
        vgp = Cnt(sem["vgp"])

        aux.inc(nc.sync.dma_start(sid[:, :], ident_dram))
        for i, nm in enumerate(("q", "k", "v")):
            aux.inc(nc.sync.dma_start(bias_s[:, NO * i:NO * (i + 1)], b_d[nm]))

        nc.vector.wait_ge(sem["aux"], 16 * 4)
        nc.vector.tensor_scalar(bias_s[:, 0:NO], bias_s[:, 0:NO],
                                SCALE, None, ALU.mult)
        dve2.inc(nc.vector.memset(ceps[:, :], EPS))
        BIAS_DONE = dve2.n

        for t in range(NT):
            if t >= 2:
                nc.sync.wait_ge(sem["ln1c"], t - 1)
            if xld.n:
                nc.sync.wait_ge(sem["xld"], xld.n)
            xld.inc(nc.sync.dma_start(xbuf[t % 2][:, :], x_d[128 * t:128 * (t + 1), :]))

        def emit_ln_stats(src_slices):
            last = None
            for c, sl in enumerate(src_slices):
                last = nc.vector.bn_stats(st6[:, 6 * c:6 * c + 6], sl)
            dve2.inc(last)
            nc.vector.wait_ge(sem["dve2"], dve2.n)
            dve2.inc(nc.vector.bn_aggr(mv[:, :], st6[:, 0:6 * len(src_slices)]))
            nc.vector.wait_ge(sem["dve2"], dve2.n)

        def emit_rstd():
            nc.scalar.wait_ge(sem["dve2"], dve2.n)
            actx.inc(nc.scalar.activation(ltmp[:, :], mv[:, 1:2], AF.Ln,
                                          bias=ceps[:, 0:1], scale=1.0))
            nc.scalar.wait_ge(sem["actx"], actx.n)
            actx.inc(nc.scalar.activation(rstd[:, :], ltmp[:, :], AF.Exp,
                                          bias=0.0, scale=-0.5))

        nc.tensor.wait_ge(sem["aux"], 16 * 4)
        for t in range(NT):
            nc.vector.wait_ge(sem["xld"], 16 * (t + 1))
            emit_ln_stats([xbuf[t % 2][:, 512 * c:512 * (c + 1)] for c in range(8)])
            emit_rstd()
            nc.vector.wait_ge(sem["actx"], actx.n)
            nc.vector.tensor_scalar(lbias[:, :], mv[:, 0:1], rstd[:, :], -1.0,
                                    ALU.mult, ALU.mult)
            if t >= 2:
                nc.vector.wait_ge(sem["lnt"], 8 * (t - 1))
            ln1c.inc(nc.vector.tensor_scalar(lntile[t % 2][:, :], xbuf[t % 2][:, :],
                                             rstd[:, :], lbias[:, :],
                                             ALU.mult, ALU.add))
            nc.tensor.wait_ge(sem["ln1c"], t + 1)
            for cb in range(8):
                B = 8 * t + cb
                bank = B % 2
                if B >= 2:
                    nc.tensor.wait_ge(sem["lne"], B - 1)
                for i in range(4):
                    c = 4 * cb + i
                    ins = nc.tensor.transpose(
                        psR[:, 1024 * bank + 128 * i:1024 * bank + 128 * (i + 1)],
                        lntile[t % 2][:, 128 * c:128 * (c + 1)], sid[:, :])
                lnt.inc(ins)
            for cb in range(8):
                B = 8 * t + cb
                bank = B % 2
                nc.scalar.wait_ge(sem["lnt"], B + 1)
                src = psR[:, 1024 * bank:1024 * bank + 512]
                dstap = ln1T[:, :].rearrange("p (c s) -> p c s", c=NC)[
                    :, 4 * cb:4 * cb + 4, 128 * t:128 * (t + 1)]
                lne.inc(nc.scalar.activation(dstap, src, AF.Copy))

        if debug:
            nc.sync.wait_ge(sem["lne"], 8 * NT)
            nc.sync.dma_start(ln1_dump, ln1T).then_inc(sem["aux"], 16)
        for w in range(96):
            if w >= NW:
                nc.gpsimd.wait_ge(sem["mmP"], NB * (w - NW + 1))
            if wdma.n:
                nc.gpsimd.wait_ge(sem["wdma"], wdma.n)
            src = w_d[("q", "k", "v")[w // 32]][w % 32].rearrange("p c o -> p (c o)")
            wdma.inc(nc.gpsimd.dma_start(wring[w % NW][:, :], src))

        g = 0
        for pi in range(3):
            for j in range(NO):
                w = pi * 32 + j
                nc.tensor.wait_ge(sem["wdma"], 16 * (w + 1))
                for b in range(NB):
                    if g >= 4:
                        nc.tensor.wait_ge(sem["evP"], g - 3)
                    nc.tensor.wait_ge(sem["lne"], 32 + 32 * b)
                    pslice = psP[:, 512 * (g % 4):512 * (g % 4 + 1)]
                    for c in range(NC):
                        ins = nc.tensor.matmul(
                            pslice,
                            wring[w % NW][:, 128 * c:128 * (c + 1)],
                            ln1T[:, 1024 * c + 512 * b:1024 * c + 512 * (b + 1)],
                            start=(c == 0), stop=(c == NC - 1))
                    mmP.inc(ins)
                    g += 1

        nc.scalar.wait_ge(sem["aux"], 16 * 4)
        nc.scalar.wait_ge(sem["dve2"], BIAS_DONE)
        nc.scalar.wait_ge(sem["lnt"], 8 * NT)
        g = 0
        nspill = 0
        for pi in range(3):
            for j in range(NO):
                for b in range(NB):
                    nc.scalar.wait_ge(sem["mmP"], g + 1)
                    pslice = psP[:, 512 * (g % 4):512 * (g % 4 + 1)]
                    bcol = bias_s[:, NO * pi + j:NO * pi + j + 1]
                    if pi == 1:
                        dstk = kT[:, 1024 * j + 512 * b:1024 * j + 512 * (b + 1)]
                        evP.inc(nc.scalar.activation(dstk, pslice, AF.Identity,
                                                     bias=bcol, scale=1.0))
                    else:
                        if nspill >= 4:
                            nc.scalar.wait_ge(sem["spP%d" % (nspill % 4)],
                                              16 * (nspill // 4))
                        eb = evac[nspill % 4]
                        sc = SCALE if pi == 0 else 1.0
                        evP.inc(nc.scalar.activation(eb[:, :], pslice, AF.Identity,
                                                     bias=bcol, scale=sc))
                        nc.scalar.wait_ge(sem["evP"], evP.n)
                        dst = (qT_d if pi == 0 else vT_d)[4 * b:4 * b + 4, :, j, :]
                        spP4[nspill % 4].inc(nc.scalar.dma_start(
                            dst.rearrange("st d s -> d st s"), eb[:, :]))
                        nspill += 1
                    g += 1

        nc.sync.wait_ge(sem["mmP"], 192)
        for e in range(4):
            nc.sync.wait_ge(sem["spP%d" % e], 16 * 32)
        for st in range(NT):
            if st >= 2:
                nc.sync.wait_ge(sem["scs"], NSG * (st - 1))
                nc.sync.wait_ge(sem["vgp"], NSG * (st - 1))
            if qvtl.n:
                nc.sync.wait_ge(sem["qvtl"], qvtl.n)
            qvtl.inc(nc.sync.dma_start(
                qtile[st % 2][:, :], qT_d[st].rearrange("d h s -> d (h s)")))
            nc.sync.wait_ge(sem["qvtl"], qvtl.n)
            qvtl.inc(nc.sync.dma_start(
                vtile[st % 2][:, :], vT_d[st].rearrange("d h s -> d (h s)")))

        nc.vector.memset(bdiagS[0][:, :], 0.0)
        dve2.inc(nc.vector.memset(bdiagS[1][:, :], 0.0))
        BDZ = dve2.n

        def dvc_recip(sg):
            return 4 * sg + 1

        def pe_scores(sg):
            st, sgi = divmod(sg, NSG)
            s0 = 16 * sgi
            par4 = sg % 4
            if sgi == 0:
                nc.tensor.wait_ge(sem["qvtl"], 16 * 2 * (st + 1))
                if st == 0:
                    nc.tensor.wait_ge(sem["evP"], 128)
                    nc.tensor.wait_ge(sem["dve2"], BDZ)
            if sg >= 4:
                nc.tensor.wait_ge(sem["exs"], sg - 3)
            scps = psP[:, 512 * par4:512 * par4 + 128]
            ins = None
            for g2 in range(4):
                for p in range(4):
                    sp_ = s0 + 4 * g2 + p
                    qs = qtile[st % 2].rearrange("d (h s) -> d s h", h=32)[:, sp_, :]
                    ks = kT.rearrange("d (t s) -> d s t", t=32)[:, 128 * st + sp_, :]
                    ins = nc.tensor.matmul(
                        scps[32 * p:32 * p + 32, 32 * g2:32 * g2 + 32], qs, ks,
                        start=True, stop=True, tile_position=(0, 32 * p))
            scs.inc(ins)

        def pe_vtrans(sg):
            par = sg % 2
            if sg >= 1:
                nc.tensor.wait_ge(sem["vgs"], sg)
            nc.tensor.wait_ge(sem["vgp"], sg + 1)
            ins = None
            for g2 in range(4):
                off = 1024 * (g2 // 2) + 128 * (g2 % 2)
                ins = nc.tensor.transpose(psR[:, off:off + 128],
                                          vgathS[par][:, 128 * g2:128 * (g2 + 1)],
                                          sid[:, :])
            vts.inc(ins)

        def pe_ctx(sg):
            par = sg % 2
            nc.tensor.wait_ge(sem["vgs"], sg + 1)
            nc.tensor.wait_ge(sem["dv"], dvc_recip(sg) + 2)
            if sg >= 2:
                nc.tensor.wait_ge(sem["stg"], sg - 1)
            cxps = psQ[:, 512 * par:512 * par + 512]
            ins = None
            for g2 in range(4):
                ins = nc.tensor.matmul(cxps[:, 128 * g2:128 * (g2 + 1)],
                                       vgrpS[par][:, 128 * g2:128 * (g2 + 1)],
                                       bdiagS[par][:, 128 * g2:128 * (g2 + 1)],
                                       start=True, stop=True)
            cxs.inc(ins)

        def pool_vgath(sg):
            st, sgi = divmod(sg, NSG)
            s0 = 16 * sgi
            par = sg % 2
            if sg >= 2:
                nc.gpsimd.wait_ge(sem["vts"], sg - 1)
            if sgi == 0:
                nc.gpsimd.wait_ge(sem["qvtl"], 16 * 2 * (st + 1))
            vsl = vtile[st % 2].rearrange("d (t s) -> d s t", t=32)[:, s0:s0 + 16, :]
            vgp.inc(nc.gpsimd.tensor_copy(vgathS[par], vsl))

        def dve_head(sg):
            par4 = sg % 4
            nc.vector.wait_ge(sem["scs"], sg + 1)
            if sg >= 4:
                nc.vector.wait_ge(sem["exs"], sg - 3)
            inap = psP[:, 512 * par4:512 * par4 + 128].rearrange(
                "p (g t) -> p g t", g=4)
            mxs.inc(nc.vector.reduce_max(
                negmax[:, 4 * par4:4 * par4 + 4].rearrange("p (g o) -> p g o", o=1),
                inap, axis=AX.X, negate=True))

        def dve_tail(sg):
            st, sgi = divmod(sg, NSG)
            s0 = 16 * sgi
            par = sg % 2
            par4 = sg % 4
            nc.vector.wait_ge(sem["exs"], sg + 1)
            dv.inc(nc.vector.reciprocal(rz[:, 4 * par4:4 * par4 + 4],
                                        zsum[:, 4 * par4:4 * par4 + 4]))
            nc.vector.wait_ge(sem["dv"], dvc_recip(sg))
            bc = rz[:, 4 * par4:4 * par4 + 4].rearrange(
                "p (g o) -> p g o", o=1).broadcast_to([128, 4, 32])
            dv.inc(nc.vector.tensor_tensor(
                probs2S[par4][:, :].rearrange("p (g t) -> p g t", g=4),
                probsS[par4][:, :].rearrange("p (g t) -> p g t", g=4),
                bc, ALU.mult))
            nc.vector.wait_ge(sem["dv"], dvc_recip(sg) + 1)
            dv.inc(nc.vector.transpose(probsTS[par4][:, :], probs2S[par4][:, :]))
            nc.vector.wait_ge(sem["dv"], dvc_recip(sg) + 2)
            if sg >= 2:
                nc.vector.wait_ge(sem["cxs"], sg - 1)
            ins = None
            for p in range(4):
                outap = bdiagS[par][32 * p:32 * p + 32, :].rearrange(
                    "q (g h) -> q g h", g=4)[:, :, 32 * p:32 * p + 32]
                ins = nc.vector.tensor_copy(
                    outap,
                    probsTS[par4][32 * p:32 * p + 32, :].rearrange(
                        "q (g h) -> q g h", g=4))
            dv.inc(ins)
            nc.vector.wait_ge(sem["vts"], sg + 1)
            invg = psR[:, :].rearrange("p (b c) -> p b c", b=2)[:, :, 0:256]
            vgs.inc(nc.vector.tensor_copy(
                vgrpS[par].rearrange("p (b c) -> p b c", b=2), invg))

        def act_stage_evac(sg):
            st, sgi = divmod(sg, NSG)
            s0 = 16 * sgi
            par = sg % 2
            nc.scalar.wait_ge(sem["cxs"], sg + 1)
            if sgi == 0 and st >= 2:
                nc.scalar.wait_ge(sem["cxst"], 16 * (st - 1))
            sout = stage[st % 2].rearrange("d (h s) -> d s h", h=32)[:, s0:s0 + 16, :]
            stg.inc(nc.scalar.activation(
                sout, psQ[:, 512 * par:512 * par + 512].rearrange(
                    "p (s h) -> p s h", s=16), AF.Copy))

        def act_exps(sg):
            par4 = sg % 4
            nc.scalar.wait_ge(sem["mxs"], sg + 1)
            if sg >= 4:
                nc.scalar.wait_ge(sem["dv"], dvc_recip(sg - 4) + 1)
            ins = None
            for g2 in range(4):
                ins = nc.scalar.activation(
                    probsS[par4][:, 32 * g2:32 * (g2 + 1)],
                    psP[:, 512 * par4 + 32 * g2:512 * par4 + 32 * (g2 + 1)],
                    AF.Exp, bias=negmax[:, 4 * par4 + g2:4 * par4 + g2 + 1],
                    scale=1.0, accum_out=zsum[:, 4 * par4 + g2:4 * par4 + g2 + 1])
            exs.inc(ins)

        for sg in range(NSUP + 1):
            if sg < NSUP:
                pool_vgath(sg)
                pe_scores(sg)
            if sg >= 1:
                pe_ctx(sg - 1)
            if sg < NSUP:
                pe_vtrans(sg)
                dve_head(sg)
            if sg >= 1:
                dve_tail(sg - 1)
            if sg < NSUP:
                act_exps(sg)
            if sg >= 1:
                act_stage_evac(sg - 1)
            if sg >= 1 and (sg - 1) % NSG == NSG - 1:
                st_done = (sg - 1) // NSG
                nc.scalar.wait_ge(sem["stg"], NSG * (st_done + 1))
                if cxst.n:
                    nc.scalar.wait_ge(sem["cxst"], cxst.n)
                cxst.inc(nc.scalar.dma_start(
                    cxT_d[st_done].transpose([1, 0, 2]), stage[st_done % 2][:, :]))

        nc.gpsimd.wait_ge(sem["scs"], NSUP)
        for c in range(4):
            kld.inc(nc.gpsimd.dma_start(kffn[:, 4096 * c:4096 * (c + 1)], kf_d[c]))

        nc.sync.wait_ge(sem["cxst"], 16 * NT)
        nc.vector.wait_ge(sem["cxst"], 16 * NT)
        nc.scalar.wait_ge(sem["cxst"], 16 * NT)
        nc.tensor.wait_ge(sem["cxst"], 16 * NT)
        nc.tensor.wait_ge(sem["kld"], 16 * 4)
        nc.gpsimd.wait_ge(sem["cxst"], 16 * NT)

        XLD0 = xld.n
        QV0 = qvtl.n
        for st in range(NT):
            if st >= 2:
                nc.sync.wait_ge(sem["p5"], 4 * (st - 2) + 2)
            nc.sync.wait_ge(sem["xld"], xld.n)
            xld.inc(nc.sync.dma_start(xbuf[st % 2][:, :],
                                      x_d[128 * st:128 * (st + 1), :]))
            nc.sync.wait_ge(sem["qvtl"], qvtl.n)
            qvtl.inc(nc.sync.dma_start(
                ctxn[st % 2][:, :], cxT_d[st].rearrange("h d s -> (h d) s"),
                transpose=True))

        VT0 = vts.n
        for st in range(NT):
            par = st % 2
            nc.vector.wait_ge(sem["qvtl"], QV0 + 16 * (st + 1))
            emit_ln_stats([ctxn[par][:, 512 * c:512 * (c + 1)] for c in range(8)])
            p5.inc(nc.vector.engine_nop())
            emit_rstd()
            nc.vector.wait_ge(sem["actx"], actx.n)
            dve2.inc(nc.vector.tensor_scalar(lbias[:, :], mv[:, 0:1], rstd[:, :],
                                             -1.0, ALU.mult, ALU.mult))
            nc.scalar.wait_ge(sem["dve2"], dve2.n)
            if st >= 2:
                nc.scalar.wait_ge(sem["p5"], 4 * (st - 2) + 4)
            actx.inc(nc.scalar.activation(ffnb[par][:, :], ctxn[par][:, :],
                                          AF.Identity, bias=lbias[:, :],
                                          scale=rstd[:, :]))
            nc.vector.wait_ge(sem["actx"], actx.n)
            nc.vector.wait_ge(sem["xld"], XLD0 + 16 * (st + 1))
            p5.inc(nc.vector.tensor_add(x1b[par][:, :], ffnb[par][:, :],
                                        xbuf[par][:, :]))
            if debug:
                nc.scalar.wait_ge(sem["p5"], 4 * st + 2)
                nc.scalar.dma_start(x1_dump[128 * st:128 * (st + 1), :],
                                    x1b[par][:, :]).then_inc(sem["aux"], 16)
                nc.vector.wait_ge(sem["aux"], 16 * (4 + 1 + 3 * st + 1))
            emit_ln_stats([x1b[par][:, 512 * c:512 * (c + 1)] for c in range(8)])
            emit_rstd()
            nc.gpsimd.wait_ge(sem["p5"], 4 * st + 2)
            nc.gpsimd.tensor_add(xr_raw[:, :], x1b[par][:, 0:512],
                                 x1b[par][:, 512:1024])
            for gf in range(2, 8):
                ins = nc.gpsimd.tensor_add(xr_raw[:, :], xr_raw[:, :],
                                           x1b[par][:, 512 * gf:512 * (gf + 1)])
            pool5.inc(ins)
            nc.vector.wait_ge(sem["actx"], actx.n)
            dve2.inc(nc.vector.tensor_scalar(lbias[:, :], mv[:, 0:1], rstd[:, :],
                                             -8.0, ALU.mult, ALU.mult))
            nc.vector.wait_ge(sem["pool5"], pool5.n)
            nc.vector.wait_ge(sem["dve2"], dve2.n)
            if st >= 1:
                nc.vector.wait_ge(sem["vts"], VT0 + 4 * st)
            p5.inc(nc.vector.tensor_scalar(xr16[:, :], xr_raw[:, :], rstd[:, :],
                                           lbias[:, :], ALU.mult, ALU.add))
            if debug:
                nc.scalar.wait_ge(sem["p5"], 4 * st + 3)
                nc.scalar.dma_start(xr_dump[128 * st:128 * (st + 1), :],
                                    xr_raw[:, :]).then_inc(sem["aux"], 16)
                nc.vector.wait_ge(sem["aux"], 16 * (4 + 1 + 3 * st + 2))
            nc.tensor.wait_ge(sem["p5"], 4 * st + 3)
            for c in range(4):
                T = 4 * st + c
                if T >= 2:
                    nc.tensor.wait_ge(sem["xte"], T - 1)
                tp = psR[:, 1024 * (c % 2):1024 * (c % 2) + 128]
                vts.inc(nc.tensor.transpose(tp, xr16[:, 128 * c:128 * (c + 1)],
                                            sid[:, :]))
                nc.scalar.wait_ge(sem["vts"], vts.n)
                if c == 0:
                    nc.scalar.wait_ge(sem["ffnmm"], 4 * st)
                xte.inc(nc.scalar.activation(xrT[:, 128 * c:128 * (c + 1)], tp,
                                             AF.Identity, bias=0.0, scale=1.0))
            nc.tensor.wait_ge(sem["xte"], 4 * (st + 1))
            for qq in range(4):
                QQ = 4 * st + qq
                qpar = qq % 2
                if QQ >= 2:
                    nc.tensor.wait_ge(sem["ffnev"], QQ - 1)
                for c in range(4):
                    for u2 in range(2):
                        base = 1024 * qpar + 512 * u2
                        ins = nc.tensor.matmul(
                            psP[:, base:base + 512],
                            xrT[:, 128 * c:128 * (c + 1)],
                            kffn[:, 4096 * c + 1024 * qq + 512 * u2:
                                 4096 * c + 1024 * qq + 512 * (u2 + 1)],
                            start=(c == 0), stop=(c == 3))
                ffnmm.inc(ins)
                nc.scalar.wait_ge(sem["ffnmm"], ffnmm.n)
                if qq == 0:
                    nc.scalar.wait_ge(sem["p5"], 4 * st + 2)
                    if st >= 2:
                        nc.scalar.wait_ge(sem["p5"], 4 * (st - 2) + 4)
                ffnev.inc(nc.scalar.activation(
                    ffnb[par][:, 1024 * qq:1024 * (qq + 1)],
                    psP[:, 1024 * qpar:1024 * qpar + 1024], AF.Copy))
            if debug:
                nc.scalar.dma_start(ffn_dump[128 * st:128 * (st + 1), :],
                                    ffnb[par][:, :]).then_inc(sem["aux"], 16)
                nc.vector.wait_ge(sem["aux"], 16 * (4 + 1 + 3 * st + 3))
            nc.vector.wait_ge(sem["ffnev"], ffnev.n)
            emit_ln_stats([ffnb[par][:, 512 * c:512 * (c + 1)] for c in range(8)])
            emit_rstd()
            nc.vector.wait_ge(sem["actx"], actx.n)
            nc.vector.tensor_scalar(lbias[:, :], mv[:, 0:1], rstd[:, :], -1.0,
                                    ALU.mult, ALU.mult)
            if st >= 2:
                nc.vector.wait_ge(sem["outst"], 16 * (st - 1))
            nc.vector.scalar_tensor_tensor(outb[par][:, :], ffnb[par][:, :],
                                           rstd[:, :], x1b[par][:, :],
                                           ALU.mult, ALU.add)
            p5.inc(nc.vector.tensor_scalar(outb[par][:, :], outb[par][:, :],
                                           lbias[:, :], None, ALU.add))
            nc.scalar.wait_ge(sem["p5"], 4 * st + 4)
            if outst.n:
                nc.scalar.wait_ge(sem["outst"], outst.n)
            outst.inc(nc.scalar.dma_start(out_d[128 * st:128 * (st + 1), :],
                                          outb[par][:, :]))

        nc.sync.wait_ge(sem["outst"], 16 * NT)
        if debug:
            nc.sync.wait_ge(sem["aux"], 16 * (4 + 1 + 3 * NT))
    return nc


_PROG_CACHE = {}


def kernel(L0_attn_hidden_states, L0_attn_wq, L0_attn_bq, L0_attn_wk, L0_attn_bk,
           L0_attn_wv, L0_attn_bv, L0_attn_wl, L0_attn_bl, L0_ffn_kernel):
    x = np.ascontiguousarray(np.asarray(L0_attn_hidden_states, dtype=np.float32)
                             .reshape(8192, 4096))

    def prep_w(w):
        w = np.asarray(w, dtype=np.float32)
        return np.ascontiguousarray(w.reshape(32, 128, 32, 128).transpose(0, 3, 2, 1))

    def prep_b(b):
        return np.ascontiguousarray(np.asarray(b, dtype=np.float32).reshape(32, 128).T)

    wq, wk, wv = prep_w(L0_attn_wq), prep_w(L0_attn_wk), prep_w(L0_attn_wv)
    bq, bk, bv = prep_b(L0_attn_bq), prep_b(L0_attn_bk), prep_b(L0_attn_bv)
    kf = np.ascontiguousarray(np.asarray(L0_ffn_kernel, dtype=np.int8)
                              .reshape(4, 128, 4096))

    if "nc" not in _PROG_CACHE:
        _PROG_CACHE["nc"] = build_program()
    nc = _PROG_CACHE["nc"]

    in_maps = [dict(x=np.ascontiguousarray(x[1024 * c:1024 * (c + 1)]),
                    wq=wq, wk=wk, wv=wv, bq=bq, bk=bk, bv=bv, kf=kf)
               for c in range(8)]
    res = run_bass_kernel_spmd(nc, in_maps, core_ids=list(range(8)))
    out = np.concatenate([np.asarray(r["out"]) for r in res.results], axis=0)
    return out.reshape(1, 8192, 4096).astype(np.float32)
